# version 05
# baseline (speedup 1.0000x reference)
import math

import numpy as np

import concourse.bass as bass
import concourse.mybir as mybir
from concourse.bass_utils import run_bass_kernel_spmd

F32 = mybir.dt.float32
BF16 = mybir.dt.bfloat16
U32 = mybir.dt.uint32
I32 = mybir.dt.int32
Alu = mybir.AluOpType
ActF = mybir.ActivationFunctionType
AX = mybir.AxisListType

B, S, D = 4, 4096, 1024
HALF = S // 2
NT_OWN = HALF // 128
NT_ALL = S // 128
NK = 32768
R = 128
E = 64
NCR = 64
KK = 8
NDC = D // 128

MAGIC = float(1.5 * 2 ** 23)
VELT = float(2 ** 22 + 1)
LN2 = float(np.log(2.0))
SUP = 2048
NSUP = NK // SUP
EPS_NW = float(1.0 / (1.0 + 1e-8))
ISQ = float(1.0 / math.sqrt(R))


def build_program(debug=False):
    nc = bass.Bass("TRN2", target_bir_lowering=False,
                   detect_race_conditions=False)

    x_in = nc.dram_tensor("x", [S, D], F32, kind="ExternalInput").ap()
    xo_in = nc.dram_tensor("xo", [HALF, D], F32, kind="ExternalInput").ap()
    A_in = nc.dram_tensor("A", [E, E], F32, kind="ExternalInput").ap()
    Bm_in = nc.dram_tensor("Bm", [D, E], F32, kind="ExternalInput").ap()
    Wi_in = nc.dram_tensor("Wi", [D, E], F32, kind="ExternalInput").ap()
    Wr_in = nc.dram_tensor("Wr", [NCR, D], F32, kind="ExternalInput").ap()
    Cp_in = nc.dram_tensor("Cp", [NCR, D * R], F32, kind="ExternalInput").ap()
    K_in = nc.dram_tensor("K", [NK, R], F32, kind="ExternalInput").ap()
    V_in = nc.dram_tensor("V", [NK, D], F32, kind="ExternalInput").ap()
    Wf_in = nc.dram_tensor("Wf", [D, D], F32, kind="ExternalInput").ap()
    bf_in = nc.dram_tensor("bfv", [1, D], F32, kind="ExternalInput").ap()
    dw_in = nc.dram_tensor("dw", [D, 32], F32, kind="ExternalInput").ap()
    da_in = nc.dram_tensor("da", [1, 1], F32, kind="ExternalInput").ap()

    out = nc.dram_tensor("out", [HALF, D], F32, kind="ExternalOutput").ap()
    attn_dram = nc.dram_tensor("attn_scr", [HALF, D], BF16, kind="Internal").ap()
    xq_dram = nc.dram_tensor("xq_scr", [128, NDC * HALF], BF16, kind="Internal").ap()
    sc_dram = nc.dram_tensor("sc_scr", [1, D * R], F32, kind="Internal").ap()
    erow_dram = nc.dram_tensor("erow_scr", [1, S], F32, kind="Internal").ap()
    zdram = nc.dram_tensor("z_scr", [1, 1], F32, kind="Internal").ap()

    dbg = {}
    if debug:
        for nm, shape, dt in [
            ("d_uy", [128, S], F32), ("d_h", [128, 8], F32),
            ("d_l", [1, S], F32), ("d_nw", [64, 1], F32),
            ("d_sc", [128, NDC * R], F32), ("d_qt", [128, HALF], BF16),
            ("d_wq", [128, NDC * D], BF16),
            ("d_tp", [128, NT_ALL * NCR], BF16),
            ("d_xnT", [128, NDC * HALF], BF16),
            ("d_i8", [128, KK * NT_OWN], U32), ("d_w8", [128, KK * NT_OWN], F32),
            ("d_ec", [128, NT_ALL], F32), ("d_rm", [1, 16], F32),
            ("d_zs", [64, 1], F32), ("d_xq", [128, NDC * HALF], BF16),
            ("d_st", [128, 8], F32),
        ]:
            dbg[nm] = nc.dram_tensor(nm, shape, dt, kind="ExternalOutput").ap()

    ctx = []

    def sbuf(name, shape, dt):
        t = nc.sbuf_tensor(name, shape, dt)
        h = t.__enter__()
        ctx.append(t)
        return h

    kT = sbuf("kT", [128, NK], BF16)
    xnT_own = sbuf("xnT_own", [128, NDC * HALF], BF16)
    xqb = sbuf("xqb", [128, 2048], BF16)
    wqT = sbuf("wqT", [128, NDC * D], BF16)
    buf_a = sbuf("buf_a", [128, 4096], F32)
    buf_vg = sbuf("buf_vg", [128, 4096], F32)
    iota2k = sbuf("iota2k", [128, 512], I32)
    xnT_t = sbuf("xnT_t", [128, NDC * 128 * 2], BF16)
    pool = sbuf("pool", [128, 5120], F32)
    qT = sbuf("qT", [128, HALF], BF16)
    chb512 = sbuf("chb512", [128, 512], I32)
    wrT = sbuf("wrT", [128, NDC * NCR], BF16)
    sh2k = sbuf("sh2k", [128, 1024], BF16)
    cst = sbuf("cst", [128, 4], F32)
    areg = sbuf("areg", [64, 128], F32)
    shamt = sbuf("shamt", [128, 2], I32)
    cmsk = sbuf("cmsk", [128, 1], U32)
    cmsk2 = sbuf("cmsk2", [128, 1], U32)
    cexp = sbuf("cexp", [128, 1], U32)
    btile = sbuf("btile", [128, 64], I32)
    qrec = sbuf("qrec", [128, 32], F32)
    e2dw = sbuf("e2dw", [128, 64], F32)
    gqs = sbuf("gqs", [64, E], F32)
    hcol = sbuf("hcol", [128, 1], F32)
    ecolf = sbuf("ecolf", [128, NT_ALL], F32)
    ecol = sbuf("ecol", [128, NT_ALL], BF16)
    nwcol = sbuf("nwcol", [64, 1], F32)
    zsc = sbuf("zsc", [64, 1], F32)
    redm = sbuf("redm", [1, 16], F32)
    st6 = sbuf("st6", [128, 12], F32)
    stv = sbuf("stv", [128, 2], F32)
    stis = sbuf("stis", [128, 2], F32)
    qamax = sbuf("qamax", [128, 32], F32)
    qei = sbuf("qei", [128, 64], I32)
    qef = sbuf("qef", [128, 32], F32)
    qt1 = sbuf("qt1", [128, 32], F32)
    qis = sbuf("qis", [128, 32], F32)
    qsc = sbuf("qsc", [128, 32], F32)
    dwt = sbuf("dwt", [128, 32], F32)
    dcol = sbuf("dcol", [128, 4], F32)
    rmax = sbuf("rmax", [128, 2], F32)
    rexp = sbuf("rexp", [128, NCR * 2], F32)
    rsum = sbuf("rsum", [128, 2], F32)
    rsumall = sbuf("rsumall", [128, NT_ALL], F32)
    wsp = sbuf("wsp", [128, 64], F32)
    p8t = sbuf("p8t", [128, KK], F32)
    v8t = sbuf("v8t", [128, KK], F32)
    i8t = sbuf("i8t", [128, KK], U32)
    w8t = sbuf("w8t", [128, KK], F32)
    t8t = sbuf("t8t", [128, KK], U32)
    selm = sbuf("selm", [128, 4], F32)
    ident = sbuf("ident", [64, E], F32)
    idint = sbuf("idint", [64, E], I32)

    qwk = buf_vg.ap()[:, 0:3072]
    io_a = pool.ap()[:, 0:2048]
    io_b = pool.ap()[:, 2048:3072]
    qout = pool.ap().bitcast(BF16)[:, 6144:7168]
    qout2 = pool.ap().bitcast(BF16)[:, 7168:8192]
    xnb = pool.ap().bitcast(BF16)[:, 8192:9216]
    xnb2 = pool.ap().bitcast(BF16)[:, 9216:10240]
    apow = pool.ap()[0:64, 0:66 * E]
    cstr = buf_a.ap()[0:64, 0:4096]
    scf = xnT_t.ap().bitcast(F32)[:, 0:1024]
    biasrep = scf
    attn_t = pool.ap()[:, 2048:3072]
    tp_res = qT.ap()
    bmwi = sh2k.ap()
    scbf = sh2k.ap()
    cand = sh2k.ap().bitcast(F32)
    lrow = buf_vg.ap()[0:1, 0:S]
    erow = buf_vg.ap()[32:33, 0:S]
    kstg = xqb.ap()[:, 0:1024]
    biasrep = scf
    attn_t = io_b[:, 0:1024]

    def sem(name):
        t = nc.semaphore(name)
        h = t.__enter__()
        ctx.append(t)
        return h

    sLK0 = sem("sLK0")
    sLK1 = sem("sLK1")
    sVK = sem("sVK")
    sTKw = sem("sTKw")
    sTK0 = sem("sTK0")
    sTK1 = sem("sTK1")
    sLC = sem("sLC")
    sGI = sem("sGI")
    sDC = sem("sDC")

    with nc.Block() as b0:
        @b0.gpsimd
        def _(g):
            g.iota(out=iota2k[:], pattern=[[1, 512]], base=0, channel_multiplier=0)
            g.iota(out=idint[:], pattern=[[1, E]], base=0,
                   channel_multiplier=-1)
            g.iota(out=chb512[:], pattern=[[512, 64], [0, 8]], base=0,
                   channel_multiplier=0)
            g.iota(out=btile[:], pattern=[[512, 64]], base=0,
                   channel_multiplier=0).then_inc(sGI, 1)

        @b0.sync
        def _(sy):
            sy.dma_start(
                out=io_b.rearrange("p (c w) -> p c w", w=128)[:, :, 0:64],
                in_=Bm_in.rearrange("(c p) e -> p c e", p=128)).then_inc(sLC, 16)
            sy.dma_start(
                out=io_b.rearrange("p (c w) -> p c w", w=128)[:, :, 64:128],
                in_=Wi_in.rearrange("(c p) e -> p c e", p=128)).then_inc(sLC, 16)
            sy.dma_start(out=dcol[:, 0:1],
                         in_=da_in.to_broadcast([128, 1])).then_inc(sLC, 16)
            sy.dma_start(out=areg[:, 0:64], in_=A_in[:, :]).then_inc(sLC, 16)
            sy.dma_start(out=areg[:, 64:128],
                         in_=A_in.rearrange("a b -> b a"),
                         allow_slow_non_contiguous=True).then_inc(sLC, 16)
            sy.dma_start(out=io_a[0:NCR, 0:D], in_=Wr_in[:, :]).then_inc(sLC, 16)
            for i in range(64):
                if i < 2:
                    sy.wait_ge(sVK, 1)
                else:
                    sy.wait_ge(sVK, 1 + (i - 1))
                sy.dma_start(
                    out=(io_a[:, 0:512] if i % 2 == 0 else io_a[:, 1024:1536])
                        .rearrange("p (g r) -> p g r", r=128),
                    in_=K_in.rearrange("(gg p) r -> p gg r", p=128)[:, 4 * i:4 * i + 4, :]
                ).then_inc(sLK0 if i % 2 == 0 else sLK1, 16)

        @b0.vector
        def _(v):
            v.memset(shamt[:, 0:1], 23)
            v.memset(shamt[:, 1:2], 16)
            v.memset(cmsk[:, 0:1], 0xFFFF0000)
            v.memset(cmsk2[:, 0:1], 0x0000FFFF)
            v.memset(cexp[:, 0:1], 0x7F800000)
            v.memset(cst[:, 0:1], -3.0 * LN2)
            v.memset(cst[:, 2:3], 1e-5)
            v.memset(cst[:, 3:4], 2.0 * LN2)
            v.memset(wsp[:], 0.0)
            v.memset(st6[:, 0:4], 0.0)
            v.wait_ge(sLC, 6 * 16)
            v.tensor_copy(out=bmwi, in_=io_b)
            v.tensor_copy(out=qwk.bitcast(BF16)[0:NCR, 0:D],
                          in_=io_a[0:NCR, 0:D]).then_inc(sVK, 1)
            v.wait_ge(sGI, 1)
            v.tensor_scalar(out=ident[:], in0=idint[:], scalar1=0.0,
                            scalar2=None, op0=Alu.is_equal)
            for i in range(64):
                v.wait_ge(sLK0 if i % 2 == 0 else sLK1, 16 * (i // 2 + 1))
                if i >= 2:
                    v.wait_ge(sTK0 if i % 2 == 0 else sTK1, 64 * (i // 2))
                src = io_a[:, 0:512] if i % 2 == 0 else io_a[:, 1024:1536]
                dst = kstg[:, (i % 2) * 512: (i % 2) * 512 + 512]
                v.tensor_copy(out=dst, in_=src).then_inc(sVK, 1)

        @b0.scalar
        def _(s):
            s.wait_ge(sVK, 1)
            for c in range(NDC):
                s.dma_start(out=wrT[:, c * NCR:(c + 1) * NCR],
                            in_=qwk.bitcast(BF16)[0:NCR, c * 128:(c + 1) * 128],
                            transpose=True).then_inc(sTKw, 16)
            for i in range(64):
                s.wait_ge(sVK, 1 + i + 1)
                src = kstg[:, (i % 2) * 512: (i % 2) * 512 + 512]
                for gg in range(4):
                    s.dma_start(
                        out=kT[:, (4 * i + gg) * 128:(4 * i + gg + 1) * 128],
                        in_=src[:, gg * 128:(gg + 1) * 128],
                        transpose=True).then_inc(sTK0 if i % 2 == 0 else sTK1, 16)
            s.wait_ge(sLC, 6 * 16)
            s.activation(out=dcol[:, 1:2], in_=dcol[:, 0:1],
                         func=ActF.Tanh, scale=4.0).then_inc(sDC, 1)

    def wnop(v):
        v.tensor_scalar(out=wsp[:], in0=wsp[:], scalar1=0.0,
                        scalar2=None, op0=Alu.mult)

    def q_pre(v, src):
        v.tensor_reduce(out=qamax.ap().rearrange("p (a b) -> p a b", b=1),
                        in_=src.rearrange("p (a b) -> p a b", b=32),
                        axis=AX.X, op=Alu.max, apply_absolute_value=True)
        v.tensor_scalar(out=qef.ap().bitcast(U32), in0=qamax.ap().bitcast(U32),
                        scalar1=cexp[:, 0:1], scalar2=None, op0=Alu.bitwise_and)
        v.tensor_scalar(out=qrec.ap().bitcast(U32), in0=qef.ap().bitcast(U32),
                        scalar1=cexp[:, 0:1], scalar2=None, op0=Alu.bitwise_xor)

    def q_exps(s, mode):
        if mode == "act":
            s.activation(out=qis[:], in_=qrec[:], func=ActF.Identity,
                         scale=dcol[:, 2:3])
            return s.activation(out=qsc[:], in_=qef[:], func=ActF.Identity,
                                scale=dcol[:, 3:4])
        else:
            s.activation(out=e2dw[:, 0:32], in_=dwt[:], func=ActF.Exp,
                         scale=-LN2, bias=cst[:, 3:4])
            return s.activation(out=e2dw[:, 32:64], in_=dwt[:], func=ActF.Exp,
                                scale=LN2, bias=cst[:, 0:1])

    def q_main(v, src, dst):
        a3 = lambda t: t.rearrange("p (a b) -> p a b", b=32)
        ist = qis.ap().rearrange("p (a b) -> p a b", b=1).to_broadcast([128, 32, 32])
        sc2 = qsc.ap().rearrange("p (a b) -> p a b", b=1).to_broadcast([128, 32, 32])
        q0 = qwk[:, 0:1024]
        q1 = qwk[:, 1024:2048]
        q2 = qwk[:, 2048:3072]
        v.tensor_tensor(out=a3(q0), in0=a3(src), in1=ist, op=Alu.mult)
        v.tensor_scalar(out=q0, in0=q0, scalar1=-12.0, scalar2=12.0,
                        op0=Alu.max, op1=Alu.min)
        v.tensor_scalar(out=q2, in0=q0, scalar1=VELT, scalar2=None, op0=Alu.mult)
        v.tensor_scalar(out=q1, in0=q0, scalar1=MAGIC, scalar2=MAGIC,
                        op0=Alu.add, op1=Alu.subtract)
        v.tensor_tensor(out=q0, in0=q2, in1=q0, op=Alu.subtract)
        v.tensor_tensor(out=q2, in0=q2, in1=q0, op=Alu.subtract)
        v.tensor_tensor(out=q2, in0=q2, in1=q1, op=Alu.subtract)
        v.tensor_tensor(out=q0, in0=q1, in1=q1, op=Alu.mult)
        v.tensor_scalar(out=q0, in0=q0, scalar1=16.0, scalar2=None,
                        op0=Alu.is_ge)
        v.tensor_tensor(out=q2, in0=q2, in1=q0, op=Alu.mult)
        v.tensor_tensor(out=q1, in0=q1, in1=q2, op=Alu.add)
        v.tensor_tensor(out=a3(dst), in0=a3(q1), in1=sc2, op=Alu.mult)

    sLW = sem("sLW")
    sAW = sem("sAW")
    sVW = sem("sVW")
    sTW = sem("sTW")

    with nc.Block() as bw:
        @bw.sync
        def _(sy):
            sy.wait_ge(sVK, 65)
            for t in range(8):
                if t >= 1:
                    sy.wait_ge(sVW, 2 * t)
                sy.dma_start(out=io_a[:, 0:1024],
                             in_=Wf_in[128 * t:128 * (t + 1), :]).then_inc(sLW, 16)
                sy.dma_start(out=io_b[:, 0:32],
                             in_=dw_in[128 * t:128 * (t + 1), :]).then_inc(sLW, 16)

        @bw.scalar
        def _(s):
            for t in range(8):
                s.wait_ge(sLW, 32 * t + 32)
                if t >= 1:
                    s.wait_ge(sVW, 2 * t - 1)
                s.activation(out=dwt[:], in_=io_b[:, 0:32],
                             func=ActF.Tanh, scale=4.0).then_inc(sAW, 1)
                s.wait_ge(sVW, 2 * t + 1)
                q_exps(s, "w").then_inc(sAW, 1)
                if t >= 1:
                    qsl = (qout if (t - 1) % 2 == 0 else qout2)[:, 0:1024]
                    for dc in range(NDC):
                        s.dma_start(
                            out=wqT[:, dc * D + 128 * (t - 1): dc * D + 128 * t],
                            in_=qsl[:, dc * 128:(dc + 1) * 128],
                            transpose=True).then_inc(sTW, 16)
            s.wait_ge(sVW, 16)
            qsl = (qout if 7 % 2 == 0 else qout2)[:, 0:1024]
            for dc in range(NDC):
                s.dma_start(out=wqT[:, dc * D + 128 * 7: dc * D + 128 * 8],
                            in_=qsl[:, dc * 128:(dc + 1) * 128],
                            transpose=True).then_inc(sTW, 16)

        @bw.vector
        def _(v):
            v.wait_ge(sTKw, 8 * 16)
            for t in range(8):
                v.wait_ge(sLW, 32 * t + 32)
                v.wait_ge(sAW, 2 * t + 1)
                v.tensor_scalar(out=dwt[:], in0=dwt[:], scalar1=0.5,
                                scalar2=None, op0=Alu.mult)
                q_pre(v, io_a[:, 0:1024])
                v.engine_nop().then_inc(sVW, 1)
                v.wait_ge(sAW, 2 * t + 2)
                v.tensor_tensor(out=qis[:], in0=qrec[:], in1=e2dw[:, 0:32],
                                op=Alu.mult)
                v.tensor_tensor(out=qsc[:], in0=qef[:], in1=e2dw[:, 32:64],
                                op=Alu.mult)
                if t >= 2:
                    v.wait_ge(sTW, 16 * NDC * (t - 1))
                q_main(v, io_a[:, 0:1024],
                       (qout if t % 2 == 0 else qout2)[:, 0:1024])
                v.engine_nop().then_inc(sVW, 1)
    TW_TOT = 16 * NDC * 8

    sL10 = sem("sL10")
    sL11 = sem("sL11")
    sV1 = sem("sV1")
    sV1e = sem("sV1e")
    sA1 = sem("sA1")
    sT10 = sem("sT10")
    sT11 = sem("sT11")
    sP1 = sem("sP1")
    sE1 = sem("sE1")
    sX1 = sem("sX1")
    sF1 = sem("sF1")

    with (nc.psum_tensor("pB0", [128, 128], F32) as pB0,
          nc.psum_tensor("pB1", [128, 128], F32) as pB1,
          nc.psum_tensor("pA0", [128, NCR], F32) as pA0,
          nc.psum_tensor("pA1", [128, NCR], F32) as pA1,
          nc.Block() as b1):

        @b1.vector
        def _(v):
            v.wait_ge(sDC, 1)
            v.tensor_scalar(out=dcol[:, 1:2], in0=dcol[:, 1:2],
                            scalar1=0.5, scalar2=None,
                            op0=Alu.mult).then_inc(sV1e, 1)
            for gt in range(NT_ALL):
                sl = io_a[:, (gt % 2) * 1024:(gt % 2) * 1024 + 1024]
                xslot = (xnb if gt % 2 == 0 else xnb2)[:, 0:1024]
                v.wait_ge(sL10 if gt % 2 == 0 else sL11, 16 * (gt // 2 + 1))
                v.tensor_tensor(out=io_b, in0=sl, in1=sl, op=Alu.mult)
                v.tensor_reduce(out=stv[:, 0:1], in_=sl, axis=AX.X, op=Alu.add)
                v.tensor_reduce(out=stv[:, 1:2], in_=io_b, axis=AX.X,
                                op=Alu.add).then_inc(sV1, 1)
                v.wait_ge(sA1, gt + 1)
                v.tensor_scalar(out=io_b, in0=sl,
                                scalar1=st6[:, 0:1], scalar2=stis[:, 1:2],
                                op0=Alu.subtract, op1=Alu.mult).then_inc(sV1, 1)
                if gt >= 2:
                    v.wait_ge(sT10 if gt % 2 == 0 else sT11, 128 * (gt // 2))
                v.tensor_copy(out=xslot, in_=io_b).then_inc(sV1, 1)
                v.wait_ge(sP1, gt + 1)
                pb = (pB0 if gt % 2 == 0 else pB1)[:]
                pa = (pA0 if gt % 2 == 0 else pA1)[:]
                v.tensor_reduce(out=rmax[:, 0:1], in_=pa, axis=AX.X, op=Alu.max)
                v.tensor_copy(out=buf_a[:, gt * 128:(gt + 1) * 128],
                              in_=pb).then_inc(sE1, 1)
                v.tensor_scalar(out=rmax[:, 1:2], in0=rmax[:, 0:1],
                                scalar1=-1.0, scalar2=None,
                                op0=Alu.mult).then_inc(sE1, 1)
                v.wait_ge(sX1, gt + 1)
                re = rexp.ap()[:, (gt % 2) * NCR:(gt % 2) * NCR + NCR]
                v.tensor_reduce(out=rsumall[:, gt:gt + 1], in_=re,
                                axis=AX.X, op=Alu.add)
                v.tensor_copy(out=tp_res[:, gt * NCR:(gt + 1) * NCR],
                              in_=re).then_inc(sF1, 1)

        @b1.scalar
        def _(s):
            s.wait_ge(sV1e, 1)
            s.activation(out=dcol[:, 2:3], in_=dcol[:, 1:2], func=ActF.Exp,
                         scale=-LN2, bias=cst[:, 3:4])
            s.activation(out=dcol[:, 3:4], in_=dcol[:, 1:2], func=ActF.Exp,
                         scale=LN2, bias=cst[:, 0:1])
            for gt in range(NT_ALL):
                s.wait_ge(sV1, 3 * gt + 1)
                s.activation(out=st6[:, 0:1], in_=stv[:, 0:1],
                             func=ActF.Identity, scale=1.0 / D)
                s.activation(out=st6[:, 1:2], in_=stv[:, 0:1],
                             func=ActF.Identity, scale=-1.0 / D)
                s.activation(out=wsp[:, 0:32], in_=wsp[:, 0:32],
                             func=ActF.Identity)
                s.activation(out=st6[:, 2:3], in_=st6[:, 0:1],
                             func=ActF.Identity, scale=st6[:, 1:2],
                             bias=cst[:, 2:3])
                s.activation(out=wsp[:, 0:32], in_=wsp[:, 0:32],
                             func=ActF.Identity)
                s.activation(out=stis[:, 0:1], in_=stv[:, 1:2],
                             func=ActF.Ln, scale=1.0 / D, bias=st6[:, 2:3])
                s.activation(out=stis[:, 1:2], in_=stis[:, 0:1],
                             func=ActF.Exp, scale=-0.5).then_inc(sA1, 1)
                s.wait_ge(sV1, 3 * gt + 3)
                xs = (xnb if gt % 2 == 0 else xnb2)[:, 0:1024]
                for dc in range(NDC):
                    s.dma_start(
                        out=xnT_t[:, (gt % 2) * NDC * 128 + dc * 128:
                                  (gt % 2) * NDC * 128 + (dc + 1) * 128],
                        in_=xs[:, dc * 128:(dc + 1) * 128],
                        transpose=True).then_inc(sT10 if gt % 2 == 0 else sT11, 16)
                s.wait_ge(sE1, 2 * gt + 2)
                pa = (pA0 if gt % 2 == 0 else pA1)[:]
                s.activation(out=rexp.ap()[:, (gt % 2) * NCR:(gt % 2) * NCR + NCR],
                             in_=pa, func=ActF.Exp,
                             bias=rmax[:, 1:2]).then_inc(sX1, 1)

        @b1.sync
        def _(sy):
            sy.wait_ge(sVW, 16)
            for gt in range(NT_ALL):
                if gt >= 2:
                    sy.wait_ge(sV1, 3 * (gt - 1) + 2)
                sy.dma_start(out=io_a[:, (gt % 2) * 1024:(gt % 2) * 1024 + 1024],
                             in_=x_in[128 * gt:128 * (gt + 1), :]
                             ).then_inc(sL10 if gt % 2 == 0 else sL11, 16)

        @b1.tensor
        def _(te):
            te.wait_ge(sTKw, 8 * 16)
            for gt in range(NT_ALL):
                te.wait_ge(sT10 if gt % 2 == 0 else sT11, 128 * (gt // 2 + 1))
                if gt >= 2:
                    te.wait_ge(sE1, 2 * (gt - 1))
                    te.wait_ge(sX1, gt - 1)
                xv = xnT_t.ap()[:, (gt % 2) * NDC * 128:(gt % 2 + 1) * NDC * 128]
                pb = (pB0 if gt % 2 == 0 else pB1)[:]
                pa = (pA0 if gt % 2 == 0 else pA1)[:]
                for dc in range(NDC):
                    te.matmul(pb, bmwi[:, dc * 128:(dc + 1) * 128],
                              xv[:, dc * 128:(dc + 1) * 128],
                              start=(dc == 0), stop=(dc == NDC - 1))
                for dc in range(NDC):
                    mm = te.matmul(pa, xv[:, dc * 128:(dc + 1) * 128],
                                   wrT[:, dc * NCR:(dc + 1) * NCR],
                                   start=(dc == 0), stop=(dc == NDC - 1))
                mm.then_inc(sP1, 1)

    sL20 = sem("sL20")
    sL21 = sem("sL21")
    sV2 = sem("sV2")
    sA2 = sem("sA2")
    sV3 = sem("sV3")
    sT20 = sem("sT20")
    sT21 = sem("sT21")
    sXQ0 = sem("sXQ0")
    sXQ1 = sem("sXQ1")

    with nc.Block() as b2:
        @b2.sync
        def _(sy):
            sy.wait_ge(sV1, 3 * NT_ALL - 1)
            for t in range(NT_OWN):
                if t >= 2:
                    sy.wait_ge(sV2, 4 * (t - 1) + 2)
                sy.dma_start(out=io_a[:, (t % 2) * 1024:(t % 2) * 1024 + 1024],
                             in_=xo_in[128 * t:128 * (t + 1), :]
                             ).then_inc(sL20 if t % 2 == 0 else sL21, 16)

        @b2.vector
        def _(v):
            v.wait_ge(sTW, TW_TOT)
            for t in range(NT_OWN):
                sl = io_a[:, (t % 2) * 1024:(t % 2) * 1024 + 1024]
                xslot = (xnb if t % 2 == 0 else xnb2)[:, 0:1024]
                qslot = (qout if t % 2 == 0 else qout2)[:, 0:1024]
                v.wait_ge(sL20 if t % 2 == 0 else sL21, 16 * (t // 2 + 1))
                v.tensor_tensor(out=io_b, in0=sl, in1=sl, op=Alu.mult)
                v.tensor_reduce(out=stv[:, 0:1], in_=sl, axis=AX.X, op=Alu.add)
                v.tensor_reduce(out=stv[:, 1:2], in_=io_b, axis=AX.X,
                                op=Alu.add).then_inc(sV2, 1)
                v.wait_ge(sA2, 2 * t + 1)
                v.tensor_scalar(out=io_b, in0=sl,
                                scalar1=st6[:, 0:1], scalar2=stis[:, 1:2],
                                op0=Alu.subtract, op1=Alu.mult).then_inc(sV2, 1)
                if t >= 2:
                    v.wait_ge(sT20 if t % 2 == 0 else sT21, 256 * (t // 2))
                v.tensor_copy(out=xslot, in_=io_b).then_inc(sV2, 1)
                q_pre(v, io_b)
                v.engine_nop().then_inc(sV2, 1)
                v.wait_ge(sA2, 2 * t + 2)
                q_main(v, io_b, qslot)
                v.engine_nop().then_inc(sV3, 1)

        @b2.scalar
        def _(s):
            for t in range(NT_OWN):
                s.wait_ge(sV2, 4 * t + 1)
                s.activation(out=st6[:, 0:1], in_=stv[:, 0:1],
                             func=ActF.Identity, scale=1.0 / D)
                s.activation(out=st6[:, 1:2], in_=stv[:, 0:1],
                             func=ActF.Identity, scale=-1.0 / D)
                s.activation(out=wsp[:, 0:32], in_=wsp[:, 0:32],
                             func=ActF.Identity)
                s.activation(out=st6[:, 2:3], in_=st6[:, 0:1],
                             func=ActF.Identity, scale=st6[:, 1:2],
                             bias=cst[:, 2:3])
                s.activation(out=wsp[:, 0:32], in_=wsp[:, 0:32],
                             func=ActF.Identity)
                s.activation(out=stis[:, 0:1], in_=stv[:, 1:2],
                             func=ActF.Ln, scale=1.0 / D, bias=st6[:, 2:3])
                s.activation(out=stis[:, 1:2], in_=stis[:, 0:1],
                             func=ActF.Exp, scale=-0.5).then_inc(sA2, 1)
                s.wait_ge(sV2, 4 * t + 4)
                q_exps(s, "act").then_inc(sA2, 1)
                xs = (xnb if t % 2 == 0 else xnb2)[:, 0:1024]
                sT2c = sT20 if t % 2 == 0 else sT21
                for dc in range(NDC):
                    s.dma_start(
                        out=xnT_own[:, dc * HALF + 128 * t: dc * HALF + 128 * (t + 1)],
                        in_=xs[:, dc * 128:(dc + 1) * 128],
                        transpose=True).then_inc(sT2c, 16)
                s.wait_ge(sV3, t + 1)
                if t >= 2:
                    s.wait_ge(sXQ0 if t % 2 == 0 else sXQ1, 16 * (t // 2))
                qs = (qout if t % 2 == 0 else qout2)[:, 0:1024]
                xb = xqb.ap()[:, (t % 2) * 1024:(t % 2) * 1024 + 1024]
                for dc in range(NDC):
                    s.dma_start(
                        out=xb[:, dc * 128:(dc + 1) * 128],
                        in_=qs[:, dc * 128:(dc + 1) * 128],
                        transpose=True).then_inc(sT2c, 16)
                s.dma_start(
                    out=xq_dram.rearrange("p (c h) -> p c h", h=HALF)[:, :, 128 * t:128 * (t + 1)],
                    in_=xb.rearrange("p (c w) -> p c w", w=128)
                    ).then_inc(sXQ0 if t % 2 == 0 else sXQ1, 16)

    sPP = sem("sPP")
    sVP = sem("sVP")
    sAP = sem("sAP")
    sDPh = sem("sDPh")
    sDPe = sem("sDPe")
    sDPc = sem("sDPc")
    sDPz = sem("sDPz")

    with (nc.psum_tensor("pS0", [64, E], F32) as pS0,
          nc.psum_tensor("pS1", [64, E], F32) as pS1,
          nc.psum_tensor("pL", [1, 2048], F32) as pL,
          nc.psum_tensor("pSh", [64, 1], F32) as pSh,
          nc.Block() as b3):

        @b3.tensor
        def _(te):
            te.wait_ge(sLC, 5 * 16)
            te.wait_ge(sVP, 1)
            for k in range(1, 64):
                if k >= 2:
                    te.wait_ge(sVP, k)
                te.matmul((pS0 if k % 2 else pS1)[:], apow[:, 65 * E:66 * E],
                          apow[:, k * E:(k + 1) * E],
                          start=True, stop=True).then_inc(sPP, 1)
            te.wait_ge(sVP, 64)
            te.wait_ge(sE1, 2 * NT_ALL)
            for k in range(64):
                mm = te.matmul(pS0[:], apow[:, (63 - k) * E:(64 - k) * E],
                               buf_a.ap()[0:64, :].rearrange("p (q k) -> p k q", k=64)[:, k, :],
                               start=(k == 0), stop=(k == 63))
            mm.then_inc(sPP, 1)
            te.transpose(pS1[:], apow[:, 64 * E:65 * E],
                         ident[:]).then_inc(sPP, 1)
            for k in range(1, 64):
                te.wait_ge(sVP, 65 + k)
                te.matmul((pS0 if k % 2 else pS1)[:], apow[:, 65 * E:66 * E],
                          apow[:, k * E:(k + 1) * E],
                          start=True, stop=True).then_inc(sPP, 1)
            te.wait_ge(sVP, 129)
            for q in range(64):
                mm = te.matmul(pSh[:], apow[:, (63 - q) * E:(64 - q) * E],
                               gqs[:, q:q + 1], start=(q == 0), stop=(q == 63))
            mm.then_inc(sPP, 1)
            te.wait_ge(sDPh, 16)
            for hh in range(2):
                if hh == 1:
                    te.wait_ge(sVP, 131)
                for j4 in range(4):
                    mm = te.matmul(pL[:, j4 * 512:(j4 + 1) * 512],
                                   hcol[64:128, 0:1],
                                   buf_a[64:128, hh * 2048 + j4 * 512: hh * 2048 + (j4 + 1) * 512],
                                   start=True, stop=True)
                mm.then_inc(sPP, 1)
            te.wait_ge(sVP, 135)
            te.wait_ge(sF1, NT_ALL)
            for t in range(NT_ALL):
                mm = te.matmul(pSh[:], tp_res[:, t * NCR:(t + 1) * NCR],
                               ecol[:, t:t + 1], start=(t == 0), stop=(t == NT_ALL - 1))
            mm.then_inc(sPP, 1)

        @b3.vector
        def _(v):
            v.wait_ge(sT20, 256 * 8)
            v.wait_ge(sT21, 256 * 8)
            v.tensor_copy(out=apow[:, 0:E], in_=ident[:])
            v.tensor_copy(out=apow[:, E:2 * E], in_=areg[:, 0:64])
            v.tensor_copy(out=apow[:, 65 * E:66 * E],
                          in_=areg[:, 64:128]).then_inc(sVP, 1)
            for k in range(1, 64):
                v.wait_ge(sPP, k)
                v.tensor_copy(out=apow[:, (k + 1) * E:(k + 2) * E],
                              in_=(pS0 if k % 2 else pS1)[:]).then_inc(sVP, 1)
            v.wait_ge(sPP, 64)
            v.tensor_copy(out=gqs[:], in_=pS0[:]).then_inc(sVP, 1)
            v.wait_ge(sPP, 65)
            v.tensor_copy(out=apow[:, 65 * E:66 * E], in_=pS1[:])
            v.tensor_copy(out=apow[:, E:2 * E],
                          in_=apow[:, 64 * E:65 * E]).then_inc(sVP, 1)
            for k in range(1, 64):
                v.wait_ge(sPP, 65 + k)
                v.tensor_copy(out=apow[:, (k + 1) * E:(k + 2) * E],
                              in_=(pS0 if k % 2 else pS1)[:]).then_inc(sVP, 1)
            v.wait_ge(sPP, 129)
            v.tensor_copy(out=hcol[0:64, 0:1], in_=pSh[:]).then_inc(sVP, 1)
            for hh in range(2):
                v.wait_ge(sPP, 130 + hh)
                v.tensor_copy(out=lrow[:, hh * 2048:(hh + 1) * 2048],
                              in_=pL[:]).then_inc(sVP, 1)
            v.tensor_reduce(out=redm.ap()[0:1, 0:8].rearrange("p (a b) -> p a b", b=1),
                            in_=lrow.rearrange("p (a b) -> p a b", b=512),
                            axis=AX.X, op=Alu.max)
            v.tensor_reduce(out=redm[0:1, 8:9], in_=redm[0:1, 0:8],
                            axis=AX.X, op=Alu.max)
            v.tensor_scalar(out=redm[0:1, 9:10], in0=redm[0:1, 8:9],
                            scalar1=-1.0, scalar2=None,
                            op0=Alu.mult).then_inc(sVP, 1)
            v.wait_ge(sAP, 1)
            v.tensor_reduce(out=redm.ap()[0:1, 0:8].rearrange("p (a b) -> p a b", b=1),
                            in_=erow.rearrange("p (a b) -> p a b", b=512),
                            axis=AX.X, op=Alu.add)
            wnop(v)
            v.tensor_reduce(out=redm[0:1, 10:11], in_=redm[0:1, 0:8],
                            axis=AX.X, op=Alu.add).then_inc(sVP, 1)
            v.wait_ge(sDPc, 16)
            v.wait_ge(sAP, 2)
            v.tensor_tensor(out=ecol[:], in0=ecolf[:], in1=rexp[:, 0:NT_ALL],
                            op=Alu.mult).then_inc(sVP, 1)
            v.wait_ge(sPP, 132)
            v.wait_ge(sDPz, 32)
            v.tensor_scalar(out=nwcol[:], in0=pSh[:],
                            scalar1=zsc[0:64, 0:1], scalar2=EPS_NW,
                            op0=Alu.mult, op1=Alu.mult).then_inc(sVP, 1)

        @b3.scalar
        def _(s):
            s.wait_ge(sVP, 133)
            s.activation(out=erow, in_=lrow, func=ActF.Exp).then_inc(sAP, 1)
            s.wait_ge(sF1, NT_ALL)
            s.activation(out=rexp[:, NT_ALL:2 * NT_ALL], in_=rsumall[:],
                         func=ActF.Ln)
            s.activation(out=rexp[:, 0:NT_ALL], in_=rexp[:, NT_ALL:2 * NT_ALL],
                         func=ActF.Exp, scale=-1.0).then_inc(sAP, 1)
            s.wait_ge(sVP, 134)
            s.activation(out=redm[0:1, 12:13], in_=redm[0:1, 10:11],
                         func=ActF.Ln)
            s.activation(out=redm[0:1, 11:12], in_=redm[0:1, 12:13],
                         func=ActF.Exp, scale=-1.0).then_inc(sAP, 1)

        @b3.gpsimd
        def _(g):
            g.wait_ge(sVP, 130)
            g.dma_start(out=hcol[64:128, 0:1],
                        in_=hcol[0:64, 0:1]).then_inc(sDPh, 16)
            g.wait_ge(sAP, 1)
            g.dma_start(out=erow_dram[:, :], in_=erow).then_inc(sDPe, 16)
            g.wait_ge(sDPe, 16)
            g.dma_start(out=ecolf[:, :],
                        in_=erow_dram.rearrange("o (c p) -> (o p) c", p=128),
                        allow_slow_non_contiguous=True).then_inc(sDPc, 16)
            g.wait_ge(sAP, 3)
            g.dma_start(out=zdram[:, :], in_=redm[0:1, 11:12]).then_inc(sDPz, 16)
            g.wait_ge(sDPz, 16)
            g.dma_start(out=zsc[0:64, 0:1],
                        in_=zdram.to_broadcast([64, 1])).then_inc(sDPz, 16)

    sCS0 = sem("sCS0")
    sCS1 = sem("sCS1")
    sEVa = sem("sEVa")
    sEVd = sem("sEVd")
    sSC0 = sem("sSC0")
    sSC1 = sem("sSC1")
    sSCg = sem("sSCg")
    sPM = sem("sPM")

    with (nc.psum_tensor("pC0", [1, 2048], F32) as pC0,
          nc.psum_tensor("pC1", [1, 2048], F32) as pC1,
          nc.Block() as b4):

        srow0 = buf_vg.ap()[64:65, 0:2048]
        srow1 = buf_vg.ap()[96:97, 0:2048]

        @b4.sync
        def _(sy):
            sy.wait_ge(sPP, 64)
            for st in range(64):
                if st >= 2:
                    sy.wait_ge(sPM, st - 1)
                sy.dma_start(
                    out=cstr[:, (st % 2) * 2048:(st % 2) * 2048 + 2048],
                    in_=Cp_in[:, st * 2048:(st + 1) * 2048]
                    ).then_inc(sCS0 if st % 2 == 0 else sCS1, 16)
                if st >= 1:
                    pv = st - 1
                    if pv % 2 == 0:
                        sy.wait_ge(sEVa, pv // 2 + 1)
                    else:
                        sy.wait_ge(sEVd, (pv + 1) // 2)
                    sy.dma_start(
                        out=sc_dram[:, pv * 2048:(pv + 1) * 2048],
                        in_=(srow0 if pv % 2 == 0 else srow1)
                        ).then_inc(sSC0 if pv % 2 == 0 else sSC1, 16)
            sy.wait_ge(sEVd, 32)
            sy.dma_start(out=sc_dram[:, 63 * 2048:64 * 2048],
                         in_=srow1).then_inc(sSC1, 16)

        @b4.scalar
        def _(s):
            for st in range(0, 64, 2):
                s.wait_ge(sPM, st + 1)
                if st >= 2:
                    s.wait_ge(sSC0, 16 * (st // 2))
                s.copy(out=srow0, in_=pC0[:]).then_inc(sEVa, 1)

        @b4.vector
        def _(v):
            for st in range(1, 64, 2):
                v.wait_ge(sPM, st + 1)
                if st >= 2:
                    v.wait_ge(sSC1, 16 * (st // 2))
                v.tensor_copy(out=srow1, in_=pC1[:]).then_inc(sEVd, 1)

        @b4.tensor
        def _(te):
            te.wait_ge(sVP, 136)
            for st in range(64):
                te.wait_ge(sCS0 if st % 2 == 0 else sCS1, 16 * (st // 2 + 1))
                if st >= 2:
                    if st % 2 == 0:
                        te.wait_ge(sEVa, st // 2)
                    else:
                        te.wait_ge(sEVd, st // 2)
                pc = (pC0 if st % 2 == 0 else pC1)[:]
                for j4 in range(4):
                    mm = te.matmul(pc[:, j4 * 512:(j4 + 1) * 512], nwcol[:],
                                   cstr[:, (st % 2) * 2048 + j4 * 512:
                                        (st % 2) * 2048 + (j4 + 1) * 512],
                                   start=True, stop=True)
                mm.then_inc(sPM, 1)

    sQV = sem("sQV")
    sQP = sem("sQP")

    with (nc.psum_tensor("pQ0", [128, 512], F32) as pQ0,
          nc.psum_tensor("pQ1", [128, 512], F32) as pQ1,
          nc.Block() as b5):

        @b5.sync
        def _(sy):
            sy.wait_ge(sSC0, 32 * 16)
            sy.wait_ge(sSC1, 32 * 16)
            sy.dma_start(out=scf.rearrange("p (dc r) -> p dc r", r=R),
                         in_=sc_dram.rearrange("o (dc p r) -> (o p) dc r",
                                               p=128, r=R)).then_inc(sSCg, 16)

        @b5.vector
        def _(v):
            v.wait_ge(sSCg, 16)
            v.tensor_copy(out=scbf, in_=scf).then_inc(sQV, 1)
            for qc in range(4):
                v.wait_ge(sQP, qc + 1)
                v.tensor_copy(out=qT.ap()[:, qc * 512:(qc + 1) * 512],
                              in_=(pQ0 if qc % 2 == 0 else pQ1)[:]).then_inc(sQV, 1)

        @b5.tensor
        def _(te):
            te.wait_ge(sQV, 1)
            te.wait_ge(sT20, 256 * 8)
            te.wait_ge(sT21, 256 * 8)
            for qc in range(4):
                if qc >= 2:
                    te.wait_ge(sQV, qc)
                pq = (pQ0 if qc % 2 == 0 else pQ1)[:]
                for dc in range(NDC):
                    mm = te.matmul(pq, scbf[:, dc * 128:(dc + 1) * 128],
                                   xnT_own[:, dc * HALF + qc * 512: dc * HALF + (qc + 1) * 512],
                                   start=(dc == 0), stop=(dc == NDC - 1))
                mm.then_inc(sQP, 1)

    if debug:
        sDB = sem("sDB")
        with nc.Block() as bdg:
            @bdg.sync
            def _(sy):
                sy.wait_ge(sQV, 5)
                sy.wait_ge(sVP, 136)
                sy.wait_ge(sTW, TW_TOT)
                sy.dma_start(out=dbg["d_uy"], in_=buf_a[:]).then_inc(sDB, 16)
                sy.dma_start(out=dbg["d_h"][:, 0:1], in_=hcol[:],
                             allow_slow_non_contiguous=True).then_inc(sDB, 16)
                sy.dma_start(out=dbg["d_l"], in_=lrow).then_inc(sDB, 16)
                sy.dma_start(out=dbg["d_nw"], in_=nwcol[:]).then_inc(sDB, 16)
                sy.dma_start(out=dbg["d_sc"], in_=scf).then_inc(sDB, 16)
                sy.dma_start(out=dbg["d_qt"], in_=qT[:]).then_inc(sDB, 16)
                sy.dma_start(out=dbg["d_wq"], in_=wqT[:]).then_inc(sDB, 16)
                sy.dma_start(out=dbg["d_xnT"], in_=xnT_own[:]).then_inc(sDB, 16)
                sy.dma_start(out=dbg["d_ec"], in_=ecolf[:]).then_inc(sDB, 16)
                sy.dma_start(out=dbg["d_rm"], in_=redm[0:1, :]).then_inc(sDB, 16)
                sy.dma_start(out=dbg["d_zs"], in_=zsc[:]).then_inc(sDB, 16)
                sy.wait_ge(sXQ0, 16 * 8)
                sy.wait_ge(sXQ1, 16 * 8)
                sy.dma_start(out=dbg["d_xq"], in_=xq_dram[:, :]).then_inc(sDB, 16)
                sy.dma_start(out=dbg["d_st"][:, 0:2], in_=stv[:]).then_inc(sDB, 16)
                sy.dma_start(out=dbg["d_st"][:, 2:5], in_=st6[:, 0:3]).then_inc(sDB, 16)
                sy.dma_start(out=dbg["d_st"][:, 5:7], in_=stis[:]).then_inc(sDB, 16)
                sy.wait_ge(sDB, 240)

    sP3 = sem("sP3")
    sD3 = sem("sD3")
    sS3 = sem("sS3")
    sG3 = sem("sG3")
    sW3 = sem("sW3")
    sO30 = sem("sO30")
    sO31 = sem("sO31")
    sDB3 = sem("sDB3")
    sSM = sem("sSM")
    sSN = sem("sSN")

    with (nc.psum_tensor("pSc0", [128, SUP], F32) as pSc0,
          nc.psum_tensor("pSc1", [128, SUP], F32) as pSc1,
          nc.Block() as b6):

        @b6.tensor
        def _(te):
            te.wait_ge(sTK0, 32 * 64)
            te.wait_ge(sTK1, 32 * 64)
            te.wait_ge(sQV, 5)
            if debug:
                te.wait_ge(sDB, 240)
            for t in range(NT_OWN):
                for sp in range(NSUP):
                    gidx = t * NSUP + sp
                    if gidx >= 2:
                        te.wait_ge(sD3, gidx - 1)
                    ps = (pSc0 if sp % 2 == 0 else pSc1)[:]
                    for c4 in range(4):
                        mm = te.matmul(ps[:, c4 * 512:(c4 + 1) * 512],
                                       qT[:, t * 128:(t + 1) * 128],
                                       kT[:, sp * SUP + c4 * 512: sp * SUP + (c4 + 1) * 512],
                                       start=True, stop=True)
                    mm.then_inc(sP3, 1)

        @b6.vector
        def _(v):
            for t in range(NT_OWN):
                for sp in range(NSUP):
                    gidx = t * NSUP + sp
                    v.wait_ge(sP3, gidx + 1)
                    ps = (pSc0 if sp % 2 == 0 else pSc1)[:]
                    pk = buf_a.ap()[:, (sp % 2) * SUP:(sp % 2) * SUP + SUP]
                    v.scalar_tensor_tensor(
                        out=pk.bitcast(U32).rearrange("p (a b) -> p a b", b=512),
                        in0=ps.bitcast(U32).rearrange("p (a b) -> p a b", b=512),
                        scalar=cmsk[:, 0:1],
                        in1=iota2k.ap().bitcast(U32)
                            .rearrange("p (a b) -> p a b", a=1)
                            .to_broadcast([128, 4, 512]),
                        op0=Alu.bitwise_and,
                        op1=Alu.bitwise_or).then_inc(sD3, 1)
                    for c4 in range(4):
                        v.max(out=cand[:, (sp * 4 + c4) * 8:(sp * 4 + c4 + 1) * 8],
                              in_=pk[:, c4 * 512:(c4 + 1) * 512])
                wnop(v)
                v.tensor_tensor(out=cand.bitcast(U32), in0=cand.bitcast(U32),
                                in1=chb512.ap().bitcast(U32),
                                op=Alu.bitwise_or)
                v.max(out=p8t[:], in_=cand)
                wnop(v)
                v.tensor_scalar(out=v8t.ap().bitcast(U32), in0=p8t.ap().bitcast(U32),
                                scalar1=cmsk[:, 0:1], scalar2=None,
                                op0=Alu.bitwise_and)
                v.tensor_scalar(out=i8t[:], in0=p8t.ap().bitcast(U32),
                                scalar1=cmsk2[:, 0:1], scalar2=None,
                                op0=Alu.bitwise_and)
                v.tensor_scalar(out=selm[:, 0:1], in0=p8t[:, 0:1],
                                scalar1=-ISQ, scalar2=None,
                                op0=Alu.mult).then_inc(sS3, 1)
                v.wait_ge(sW3, 3 * t + 1)
                v.tensor_reduce(out=selm[:, 1:2], in_=w8t[:], axis=AX.X,
                                op=Alu.add).then_inc(sSM, 1)
                v.wait_ge(sSN, t + 1)
                v.tensor_scalar(out=w8t[:], in0=w8t[:], scalar1=selm[:, 2:3],
                                scalar2=None, op0=Alu.mult)
                wnop(v)
                v.wait_ge(sG3, (t * 8 + 4) * 16)
                v.tensor_scalar(out=attn_t, in0=buf_vg[:, 0:1024],
                                scalar1=w8t[:, 0:1], scalar2=None, op0=Alu.mult)
                for j in range(1, 4):
                    v.scalar_tensor_tensor(out=attn_t,
                                           in0=buf_vg[:, j * 1024:(j + 1) * 1024],
                                           scalar=w8t[:, j:j + 1], in1=attn_t,
                                           op0=Alu.mult, op1=Alu.add)
                v.engine_nop().then_inc(sW3, 1)
                v.wait_ge(sG3, (t * 8 + 8) * 16)
                for j in range(4, 8):
                    v.scalar_tensor_tensor(out=attn_t,
                                           in0=buf_vg[:, (j - 4) * 1024:(j - 3) * 1024],
                                           scalar=w8t[:, j:j + 1], in1=attn_t,
                                           op0=Alu.mult, op1=Alu.add)
                if t >= 2:
                    v.wait_ge(sO30 if t % 2 == 0 else sO31, 16 * (t // 2))
                v.tensor_copy(out=(qout if t % 2 == 0 else qout2)[:, 0:D],
                              in_=attn_t).then_inc(sW3, 1)

        @b6.scalar
        def _(s):
            for t in range(NT_OWN):
                s.wait_ge(sS3, t + 1)
                if t >= 1:
                    s.wait_ge(sW3, 3 * t)
                s.activation(out=w8t[:], in_=v8t[:], func=ActF.Exp,
                             scale=ISQ, bias=selm[:, 0:1]).then_inc(sW3, 1)
                s.wait_ge(sSM, t + 1)
                s.activation(out=selm[:, 3:4], in_=selm[:, 1:2], func=ActF.Ln)
                s.activation(out=selm[:, 2:3], in_=selm[:, 3:4],
                             func=ActF.Exp, scale=-1.0).then_inc(sSN, 1)

        @b6.gpsimd
        def _(g):
            for t in range(NT_OWN):
                g.wait_ge(sS3, t + 1)
                if t >= 1:
                    g.wait_ge(sW3, 3 * t)
                for j in range(4):
                    g.indirect_dma_start(
                        out=buf_vg[:, j * 1024:(j + 1) * 1024],
                        out_offset=None, in_=V_in[:, :],
                        in_offset=bass.IndirectOffsetOnAxis(i8t[:, j:j + 1], 0),
                    ).then_inc(sG3, 16)
                g.wait_ge(sW3, 3 * t + 2)
                for j in range(4, 8):
                    g.indirect_dma_start(
                        out=buf_vg[:, (j - 4) * 1024:(j - 3) * 1024],
                        out_offset=None, in_=V_in[:, :],
                        in_offset=bass.IndirectOffsetOnAxis(i8t[:, j:j + 1], 0),
                    ).then_inc(sG3, 16)

        @b6.sync
        def _(sy):
            for t in range(NT_OWN):
                sy.wait_ge(sW3, 3 * t + 3)
                sy.dma_start(out=attn_dram[128 * t:128 * (t + 1), :],
                             in_=(qout if t % 2 == 0 else qout2)[:, 0:D]
                             ).then_inc(sO30 if t % 2 == 0 else sO31, 16)
                if debug:
                    sy.dma_start(out=dbg["d_i8"][:, t * KK:(t + 1) * KK],
                                 in_=i8t[:]).then_inc(sDB3, 16)
                    sy.dma_start(out=dbg["d_w8"][:, t * KK:(t + 1) * KK],
                                 in_=w8t[:]).then_inc(sDB3, 16)
            sy.wait_ge(sO30, 16 * 8)
            sy.wait_ge(sO31, 16 * 8)
            if debug:
                sy.wait_ge(sDB3, 32 * NT_OWN)

    sL40 = sem("sL40")
    sL41 = sem("sL41")
    sLB = sem("sLB")
    sP4 = sem("sP4")
    sV4 = sem("sV4")
    sO4 = sem("sO4")

    with (nc.psum_tensor("pF0", [128, 1024], F32) as pF0,
          nc.psum_tensor("pF1", [128, 1024], F32) as pF1,
          nc.Block() as b7):

        @b7.sync
        def _(sy):
            sy.dma_start(out=biasrep, in_=bf_in.to_broadcast([128, D])
                         ).then_inc(sLB, 16)
            sy.wait_ge(sXQ0, 16 * 8)
            sy.wait_ge(sXQ1, 16 * 8)
            for t in range(NT_OWN):
                if t >= 1:
                    sy.wait_ge(sV4, t)
                    sy.dma_start(out=out[128 * (t - 1):128 * t, :],
                                 in_=io_b).then_inc(sO4, 16)
                if t >= 2:
                    sy.wait_ge(sV4, t - 1)
                    sy.wait_ge(sP4, t - 1)
                sL4c = sL40 if t % 2 == 0 else sL41
                sy.dma_start(out=io_a[:, (t % 2) * 1024:(t % 2) * 1024 + 1024],
                             in_=xo_in[128 * t:128 * (t + 1), :]).then_inc(sL4c, 16)
                sy.dma_start(out=(qout if t % 2 == 0 else qout2)[:, 0:D],
                             in_=attn_dram[128 * t:128 * (t + 1), :]).then_inc(sL4c, 16)
                sy.dma_start(out=xqb.ap()[:, (t % 2) * 1024:(t % 2) * 1024 + 1024],
                             in_=xq_dram.rearrange("p (c h) -> p c h", h=HALF)
                                 [:, :, 128 * t:128 * (t + 1)]).then_inc(sL4c, 16)
            sy.wait_ge(sV4, NT_OWN)
            sy.dma_start(out=out[128 * (NT_OWN - 1):128 * NT_OWN, :],
                         in_=io_b).then_inc(sO4, 16)
            sy.wait_ge(sO4, 16 * NT_OWN)

        @b7.tensor
        def _(te):
            te.wait_ge(sTW, TW_TOT)
            for t in range(NT_OWN):
                if t >= 2:
                    te.wait_ge(sV4, t - 1)
                te.wait_ge(sL40 if t % 2 == 0 else sL41, 48 * (t // 2 + 1))
                pf = (pF0 if t % 2 == 0 else pF1)[:]
                xb = xqb.ap()[:, (t % 2) * 1024:(t % 2) * 1024 + 1024]
                for dc in range(NDC):
                    for nh in range(2):
                        mm = te.matmul(pf[:, nh * 512:(nh + 1) * 512],
                                       xb[:, dc * 128:(dc + 1) * 128],
                                       wqT[:, dc * D + nh * 512: dc * D + (nh + 1) * 512],
                                       start=(dc == 0), stop=(dc == NDC - 1))
                mm.then_inc(sP4, 1)

        @b7.vector
        def _(v):
            for t in range(NT_OWN):
                v.wait_ge(sP4, t + 1)
                v.wait_ge(sL40 if t % 2 == 0 else sL41, 48 * (t // 2 + 1))
                v.wait_ge(sLB, 16)
                if t >= 1:
                    v.wait_ge(sO4, 16 * t)
                pf = (pF0 if t % 2 == 0 else pF1)[:]
                v.tensor_tensor(out=io_b, in0=pf, in1=biasrep, op=Alu.add)
                v.tensor_tensor(out=io_b, in0=io_b,
                                in1=(qout if t % 2 == 0 else qout2)[:, 0:D],
                                op=Alu.add)
                v.tensor_tensor(out=io_b, in0=io_b,
                                in1=io_a[:, (t % 2) * 1024:(t % 2) * 1024 + 1024],
                                op=Alu.add).then_inc(sV4, 1)

    for t in reversed(ctx):
        try:
            t.__exit__(None, None, None)
        except Exception:
            pass
    return nc


_NC_CACHE = {}


def _get_program(debug=False):
    key = bool(debug)
    if key not in _NC_CACHE:
        _NC_CACHE[key] = build_program(key)
    return _NC_CACHE[key]


def kernel(L0_attn_x, L0_attn_A, L0_attn_Bm, L0_attn_W_imp, L0_attn_W_router,
           L0_attn_compress, L0_attn_K, L0_attn_V, L0_ffn_weight, L0_ffn_bias,
           L0_ffn_delta_w, L0_ffn_delta_act, _debug=False):
    x = np.ascontiguousarray(np.asarray(L0_attn_x, dtype=np.float32))
    A = np.ascontiguousarray(np.asarray(L0_attn_A, dtype=np.float32))
    Bm = np.ascontiguousarray(np.asarray(L0_attn_Bm, dtype=np.float32))
    Wi = np.ascontiguousarray(np.asarray(L0_attn_W_imp, dtype=np.float32))
    Wr = np.ascontiguousarray(np.asarray(L0_attn_W_router, dtype=np.float32))
    Cp = np.ascontiguousarray(
        np.asarray(L0_attn_compress, dtype=np.float32).reshape(NCR, D * R))
    K = np.ascontiguousarray(np.asarray(L0_attn_K, dtype=np.float32))
    V = np.ascontiguousarray(np.asarray(L0_attn_V, dtype=np.float32))
    Wf = np.ascontiguousarray(np.asarray(L0_ffn_weight, dtype=np.float32))
    bfv = np.ascontiguousarray(np.asarray(L0_ffn_bias, dtype=np.float32).reshape(1, D))
    dw = np.ascontiguousarray(np.asarray(L0_ffn_delta_w, dtype=np.float32))
    da = np.ascontiguousarray(np.asarray(L0_ffn_delta_act, dtype=np.float32).reshape(1, 1))

    nc = _get_program(_debug)
    in_maps = []
    for c in range(8):
        b, h = c // 2, c % 2
        in_maps.append({
            "x": x[b],
            "xo": np.ascontiguousarray(x[b, h * HALF:(h + 1) * HALF]),
            "A": A, "Bm": Bm, "Wi": Wi, "Wr": Wr, "Cp": Cp,
            "K": K, "V": V, "Wf": Wf, "bfv": bfv, "dw": dw, "da": da,
        })
    res = run_bass_kernel_spmd(nc, in_maps, core_ids=list(range(8)))
    outp = np.empty((B, S, D), dtype=np.float32)
    for c in range(8):
        b, h = c // 2, c % 2
        outp[b, h * HALF:(h + 1) * HALF] = res.results[c]["out"]
    if _debug:
        return outp, res
    return outp
